# version 02 (final)
# baseline (speedup 1.0000x reference)
import numpy as np
import ml_dtypes
from contextlib import ExitStack
import concourse.bass as bass
import concourse.mybir as mybir
from concourse.bass_utils import run_bass_kernel_spmd
from concourse.bass_core_rust import IndirectOffsetOnAxis

F32 = mybir.dt.float32
BF16 = mybir.dt.bfloat16
I32 = mybir.dt.int32
ALU = mybir.AluOpType
AF = mybir.ActivationFunctionType
AX = mybir.AxisListType.X

NCORES = 8
TLOC = 4096
D = 1024
E = 16
ISH = 2048
CSEL = 4096
CAP = 640
NSC = CAP // 128
NBIS = 42
EPS_LN = 1e-5
EPS_ATT = 1e-6
ATT_SCALE = 0.125
OOBENC = 16383.0
NLIST = 128 * 80
BF = ml_dtypes.bfloat16


class S:

    def __init__(self, nc, es, names):
        self.h, self.c = {}, {}
        for n in names:
            self.h[n] = es.enter_context(nc.semaphore(n))
            self.c[n] = 0

    def inc(self, inst, name, n=1):
        inst.then_inc(self.h[name], n)
        self.c[name] += n
        return self.c[name]

    def d(self, eng, out, in_, name="LS", **kw):
        return self.inc(eng.dma_start(out=out, in_=in_, **kw), name, 16)

    def idma(self, eng, out, out_off, in_, in_off, name="LG", **kw):
        return self.inc(
            eng.indirect_dma_start(out=out, out_offset=out_off, in_=in_,
                                   in_offset=in_off, **kw), name, 16)

    def tr(self, eng, out, in_, name="LS"):
        return self.inc(eng.dma_start(out=out, in_=in_, transpose=True), name, 16)

    def fence(self, eng, inst, key="FEV"):
        self.c[key] += 1
        inst.then_inc(self.h[key], 1)
        eng.wait_ge(self.h[key], self.c[key])

    def mark(self, name):
        return self.c[name]

    def w(self, eng, name, val):
        if val > 0:
            eng.wait_ge(self.h[name], val)


def _sb(nc, es, name, shape, dtype):
    return es.enter_context(nc.sbuf_tensor(name, shape, dtype))


def _psum(nc, es, name, shape, dtype=F32):
    return es.enter_context(nc.psum_tensor(name, shape, dtype))



def emit_ln(nc, es, block, s, x_d, ln_d, lo_d, pre=None):
    xs = [_sb(nc, es, "emlx0", [128, D], F32), _sb(nc, es, "emlx1", [128, D], F32)]
    hi = [_sb(nc, es, "emlh0", [128, D], BF16), _sb(nc, es, "emlh1", [128, D], BF16)]
    lo = [_sb(nc, es, "emll0", [128, D], BF16), _sb(nc, es, "emll1", [128, D], BF16)]
    scr = _sb(nc, es, "emls", [128, D], F32)
    ln32 = _sb(nc, es, "eml32", [128, D], F32)
    m = _sb(nc, es, "emlm", [128, 1], F32)
    var = _sb(nc, es, "emlv", [128, 1], F32)
    rstd = _sb(nc, es, "emlr", [128, 1], F32)

    EX = ("EX0", "EX1")
    ES = ("ES0", "ES1")
    ex0 = [s.mark(EX[0]), s.mark(EX[1])]
    es0 = [s.mark(ES[0]), s.mark(ES[1])]
    cv0, ca0 = s.mark("CV"), s.mark("CA")
    ld_mark = {}

    @block.sync
    def _(sy):
        if pre is not None:
            for j in range(0, len(pre), 2):
                s.w(sy, pre[j], pre[j + 1])
        for c in range(32):
            b = c % 2
            if c >= 2:
                s.w(sy, "CV", cv0 + 2 * (c - 2) + 2)
            ld_mark[c] = s.d(sy, xs[b][:, :], x_d[c * 128:(c + 1) * 128, :], EX[b])

    @block.scalar
    def _(sc):
        for c in range(32):
            b = c % 2
            s.w(sc, EX[b], ld_mark[c])
            if c >= 1:
                s.w(sc, "CV", cv0 + 2 * (c - 1) + 1)
            i = sc.activation(out=scr[:, :], in_=xs[b][:, :], func=AF.Square)
            s.inc(i, "CA", 1)
            s.w(sc, "CV", cv0 + 2 * c + 1)
            i = sc.activation(out=var[:, :], in_=var[:, :], func=AF.Sqrt)
            s.inc(i, "CA", 1)
    assert s.mark("CA") == ca0 + 64

    @block.vector
    def _(v):
        for c in range(32):
            b = c % 2
            s.w(v, EX[b], ld_mark[c])
            s.fence(v, v.reduce_sum(m[:, :], xs[b][:, :], axis=AX))
            s.fence(v, v.tensor_scalar_mul(m[:, :], m[:, :], 1.0 / D))
            s.w(v, "CA", ca0 + 2 * c + 1)
            s.fence(v, v.reduce_sum(var[:, :], scr[:, :], axis=AX))
            s.fence(v, v.tensor_scalar_mul(var[:, :], var[:, :], 1.0 / D))
            s.fence(v, v.tensor_mul(rstd[:, :], m[:, :], m[:, :]))
            s.fence(v, v.tensor_sub(var[:, :], var[:, :], rstd[:, :]))
            i = v.tensor_scalar_add(var[:, :], var[:, :], EPS_LN)
            i.then_inc(s.h["CV"], 1)
            v.wait_ge(s.h["CA"], ca0 + 2 * c + 2)
            s.fence(v, v.reciprocal(rstd[:, :], var[:, :]))
            s.fence(v, v.tensor_scalar(ln32[:, :], xs[b][:, :], m[:, :],
                                       rstd[:, :], ALU.subtract, ALU.mult))
            if c >= 2:
                v.wait_ge(s.h[ES[b]], es0[b] + 32 * (c // 2))
            s.fence(v, v.tensor_copy(hi[b][:, :], ln32[:, :]))
            i = v.scalar_tensor_tensor(lo[b][:, :], ln32[:, :], 1.0, hi[b][:, :],
                                       ALU.mult, ALU.subtract)
            i.then_inc(s.h["CV"], 1)
    s.c["CV"] = cv0 + 64

    @block.gpsimd
    def _(g):
        for c in range(32):
            b = c % 2
            s.w(g, "CV", cv0 + 2 * c + 2)
            s.d(g, ln_d[c * 128:(c + 1) * 128, :], hi[b][:, :], ES[b])
            s.d(g, lo_d[c * 128:(c + 1) * 128, :], lo[b][:, :], ES[b])
    return ("ES0", s.mark("ES0"), "ES1", s.mark("ES1"))


def emit_gate(nc, es, block, s, ln_d, lo_d, gwT, p_out, ready_st):
    hiF = _sb(nc, es, "g_hiF", [128, 8 * TLOC], BF16)
    loF = _sb(nc, es, "g_loF", [128, 8 * TLOC], BF16)
    gw32 = _sb(nc, es, "g_gw32", [128, 8 * E], F32)
    gwhi = _sb(nc, es, "g_gwhi", [128, 8 * E], BF16)
    gwlo = _sb(nc, es, "g_gwlo", [128, 8 * E], BF16)
    lgt = _sb(nc, es, "g_lgt", [128, 32 * E], F32)
    psm = _psum(nc, es, "g_psm", [128, E])
    expv = [_sb(nc, es, "g_expv0", [128, E], F32),
            _sb(nc, es, "g_expv1", [128, E], F32)]
    ssum = _sb(nc, es, "g_ssum", [128, 1], F32)

    @block.sync
    def _(sy):
        if isinstance(ready_st, tuple):
            for j in range(0, len(ready_st), 2):
                s.w(sy, ready_st[j], ready_st[j + 1])
        else:
            s.w(sy, "ST", ready_st)
        for dc in range(8):
            s.tr(sy, hiF[:, dc * TLOC:(dc + 1) * TLOC],
                 ln_d[:, dc * 128:(dc + 1) * 128], "LS")
            s.tr(sy, loF[:, dc * TLOC:(dc + 1) * TLOC],
                 lo_d[:, dc * 128:(dc + 1) * 128], "LS")
        s.d(sy, gw32.rearrange("p (dc e) -> p dc e", dc=8),
            gwT.rearrange("(dc p) e -> p dc e", p=128), "LS")
    tr_done = s.mark("LS")

    cv0 = s.mark("CV")
    ct0 = s.mark("CT")

    @block.vector
    def _(v):
        s.w(v, "LS", tr_done)
        s.fence(v, v.tensor_copy(gwhi[:, :], gw32[:, :]))
        i = v.scalar_tensor_tensor(gwlo[:, :], gw32[:, :], 1.0, gwhi[:, :],
                                   ALU.mult, ALU.subtract)
        i.then_inc(s.h["CV"], 1)
        for tc in range(32):
            v.wait_ge(s.h["CT"], ct0 + tc + 1)
            i = v.tensor_copy(lgt[:, tc * E:(tc + 1) * E], psm[:, :])
            i.then_inc(s.h["CV"], 1)
    s.c["CV"] = cv0 + 33

    @block.tensor
    def _(t):
        s.w(t, "LS", tr_done)
        s.w(t, "CV", cv0 + 1)
        for tc in range(32):
            if tc >= 1:
                s.w(t, "CV", cv0 + 1 + tc)
            for dc in range(8):
                c0 = dc * TLOC + tc * 128
                gsl = slice(dc * E, (dc + 1) * E)
                t.matmul(psm[:, :], hiF[:, c0:c0 + 128], gwhi[:, gsl],
                         start=(dc == 0), stop=False)
                t.matmul(psm[:, :], hiF[:, c0:c0 + 128], gwlo[:, gsl],
                         start=False, stop=False)
                i = t.matmul(psm[:, :], loF[:, c0:c0 + 128], gwhi[:, gsl],
                             start=False, stop=(dc == 7))
            s.inc(i, "CT", 1)
    assert s.mark("CT") == ct0 + 32

    ca0 = s.mark("CA")

    @block.scalar
    def _(sc):
        s.w(sc, "CV", cv0 + 33)
        i = sc.activation(out=lgt[:, :], in_=lgt[:, :], func=AF.Exp)
        s.inc(i, "CA", 1)

    cvB = s.mark("CV")
    stB = [s.mark("ST0"), s.mark("ST1")]

    @block.vector
    def _(v):
        s.w(v, "CA", ca0 + 1)
        for tc in range(32):
            if tc >= 2:
                bb = tc % 2
                v.wait_ge(s.h["ST%d" % bb], stB[bb] + 16 * ((tc - 2) // 2 + 1))
            s.fence(v, v.reduce_sum(ssum[:, :], lgt[:, tc * E:(tc + 1) * E],
                                    axis=AX))
            s.fence(v, v.reciprocal(ssum[:, :], ssum[:, :]))
            i = v.tensor_scalar(expv[tc % 2][:, :], lgt[:, tc * E:(tc + 1) * E],
                                ssum[:, :], None, ALU.mult)
            i.then_inc(s.h["CV"], 1)
    s.c["CV"] = cvB + 32

    @block.sync
    def _(sy):
        for tc in range(32):
            s.w(sy, "CV", cvB + tc + 1)
            s.d(sy, p_out[tc * 128:(tc + 1) * 128, :], expv[tc % 2][:, :],
                "ST%d" % (tc % 2))
    return ("ST0", s.mark("ST0"), "ST1", s.mark("ST1"))



def build_L1():
    nc = bass.Bass("TRN2", target_bir_lowering=False)
    x0 = nc.dram_tensor("x0", [TLOC, D], F32, kind="ExternalInput").ap()
    gwT = nc.dram_tensor("gwT", [D, E], F32, kind="ExternalInput").ap()
    p1 = nc.dram_tensor("p1", [TLOC, E], F32, kind="ExternalOutput").ap()
    ln1 = nc.dram_tensor("ln1", [TLOC, D], BF16, kind="ExternalOutput").ap()
    lo1 = nc.dram_tensor("lo1", [TLOC, D], BF16, kind="Internal").ap()

    with ExitStack() as es:
        s = S(nc, es, ["LS", "CV", "CA", "CT", "ST", "ST0", "ST1", "FEV", "FEG", "EX0", "EX1", "ES0", "ES1"])
        with nc.Block() as block:
            with ExitStack() as es1:
                ln_done = emit_ln(nc, es1, block, s, x0, ln1, lo1)
            with ExitStack() as es2:
                st_fin = emit_gate(nc, es2, block, s, ln1, lo1, gwT, p1, ln_done)

            @block.sync
            def _(sy):
                for j in range(0, len(st_fin), 2):
                    s.w(sy, st_fin[j], st_fin[j + 1])
                for nm in ("LS", "ES0", "ES1", "EX0", "EX1"):
                    s.w(sy, nm, s.mark(nm))
    return nc



class _StopEmit(Exception):
    def __init__(self, s, block):
        self.s = s
        self.block = block


def build_FFN(with_qkv, stop_after=99):
    nc = bass.Bass("TRN2", target_bir_lowering=False)
    x_in = nc.dram_tensor("x_in", [TLOC, D], F32, kind="ExternalInput").ap()
    ln_tm = nc.dram_tensor("ln_tm", [TLOC, D], BF16, kind="ExternalInput").ap()
    b_in = nc.dram_tensor("b_in", [128, TLOC], F32, kind="ExternalInput").ap()
    b_own = nc.dram_tensor("b_own", [16, TLOC], F32, kind="ExternalInput").ap()
    s_tm_in = nc.dram_tensor("s_tm", [128, 32 * E], F32, kind="ExternalInput").ap()
    wg_til = nc.dram_tensor("wg_til", [E, 32, 128, 8 * 128], F32, kind="ExternalInput").ap()
    wu_til = nc.dram_tensor("wu_til", [E, 32, 128, 8 * 128], F32, kind="ExternalInput").ap()
    wd_til = nc.dram_tensor("wd_til", [E, 32, 128, 8 * 128], F32, kind="ExternalInput").ap()
    sgT = nc.dram_tensor("sgT", [128, 8, ISH], F32, kind="ExternalInput").ap()
    suT = nc.dram_tensor("suT", [128, 8, ISH], F32, kind="ExternalInput").ap()
    sdT = nc.dram_tensor("sdT", [128, 16 * D], F32, kind="ExternalInput").ap()
    ids32 = nc.dram_tensor("ids32", [128, 32], F32, kind="ExternalInput").ap()
    ebase = nc.dram_tensor("ebase", [128, 1], F32, kind="ExternalInput").ap()
    g8 = nc.dram_tensor("g8", [128, 128], F32, kind="ExternalInput").ap()
    ident = nc.dram_tensor("ident", [128, 128], F32, kind="ExternalInput").ap()
    x_next = nc.dram_tensor("x_next", [TLOC, D], F32, kind="ExternalOutput").ap()
    if with_qkv:
        wqT = nc.dram_tensor("wqT", [128, 8 * D], F32, kind="ExternalInput").ap()
        wkT = nc.dram_tensor("wkT", [128, 8 * D], F32, kind="ExternalInput").ap()
        wvT = nc.dram_tensor("wvT", [128, 8 * D], F32, kind="ExternalInput").ap()
        q_f = nc.dram_tensor("q_f", [D, TLOC], BF16, kind="ExternalOutput").ap()
        k_f = nc.dram_tensor("k_f", [D, TLOC], BF16, kind="ExternalOutput").ap()
        v_tm = nc.dram_tensor("v_tm", [TLOC, D], BF16, kind="ExternalOutput").ap()
        ln2_d = nc.dram_tensor("ln2_d", [TLOC, D], BF16, kind="Internal").ap()
        lo2_d = nc.dram_tensor("lo2_d", [TLOC, D], BF16, kind="Internal").ap()

    acc = nc.dram_tensor("acc_d", [TLOC, D], F32, kind="Internal").ap()
    lst = nc.dram_tensor("lst_d", [NLIST, 2], F32, kind="Internal").ap()
    xsel_d = [nc.dram_tensor(f"xsel{i}", [CAP, D], BF16, kind="Internal").ap()
              for i in range(2)]

    es = ExitStack()
    with es:
        s = S(nc, es, ["LS", "LG", "CV", "CA", "CT", "ST", "ST0", "ST1", "FEV", "FEG", "EX0", "EX1", "ES0", "ES1", "AC0", "AC1", "SC", "WT0", "WT1", "WD", "GA0", "GA1", "SH0", "SH1", "TR0", "TR1", "LSF0", "LSF1", "KV0", "KV1", "XL0", "XL1"])
        _scopes = []
        Hbuf = _sb(nc, es, "Hbuf", [128, 32 * CAP], BF16)
        lst_sb = _sb(nc, es, "lst_sb", [128, 160], F32)
        idx_i = _sb(nc, es, "idx_i", [128, 80], I32)
        w_s = _sb(nc, es, "w_s", [128, 80], F32)
        g8s = _sb(nc, es, "g8s", [128, 128], F32)
        ids_s = _sb(nc, es, "ids_s", [128, 32], F32)
        eb_s = _sb(nc, es, "eb_s", [16, 1], F32)
        idn = _sb(nc, es, "idn", [128, 128], F32)
        zed = _sb(nc, es, "zed", [128, 160], F32)
        es_ln = ExitStack(); _scopes.append(es_ln)
        lnF = _sb(nc, es_ln, "lnF", [128, 8 * TLOC], BF16)

        es1 = ExitStack(); _scopes.append(es1)
        bsc = _sb(nc, es1, "bsc", [128, TLOC], F32)
        bjunk = Hbuf
        lo_t = _sb(nc, es1, "lo_t", [128, 1], F32)
        hi_t = _sb(nc, es1, "hi_t", [128, 1], F32)
        mid_t = _sb(nc, es1, "mid_t", [128, 1], F32)
        cnt_t = _sb(nc, es1, "cnt_t", [128, 1], F32)
        cnt128 = _sb(nc, es1, "cnt128", [128, 1], F32)
        cond_f = _sb(nc, es1, "cond_f", [128, 1], F32)
        tmp_t = _sb(nc, es1, "tmp_t", [128, 1], F32)
        ow_s = _sb(nc, es1, "ow_s", [16, TLOC], F32)
        ow_m = _sb(nc, es1, "ow_m", [16, TLOC], F32)
        ow_p = _sb(nc, es1, "ow_p", [16, TLOC], F32)
        ow_t = _sb(nc, es1, "ow_t", [16, TLOC], F32)
        enc = ow_t
        stm = _sb(nc, es1, "stm", [128, 32 * E], F32)
        rows_tm = _sb(nc, es1, "rows_tm", [128, 32 * E], F32)
        rows_i = _sb(nc, es1, "rows_i", [128, 32 * E], I32)
        pay = _sb(nc, es1, "pay", [128, 32 * E * 2], F32)
        ri_a = _sb(nc, es1, "ri_a", [128, 32 * E], I32)
        ri_b = _sb(nc, es1, "ri_b", [128, 32 * E], I32)
        psb = _psum(nc, es1, "psb", [128, 1])
        rps = [_psum(nc, es1, f"rps{i}", [128, 16]) for i in range(2)]

        with nc.Block() as block:
          try:
            @block.sync
            def _(sy):
                s.d(sy, bsc[:, :], b_in[:, :], "LS")
                s.d(sy, ow_s[:, :], b_own[:, :], "LS")
                s.d(sy, stm[:, :], s_tm_in[:, :], "LS")
                s.d(sy, g8s[:, :], g8[:, :], "LS")
                s.d(sy, ids_s[:, :], ids32[:, :], "LS")
                s.d(sy, eb_s[:, :], ebase[0:16, :], "LS")
                s.d(sy, idn[:, :], ident[:, :], "LS")
                for dc in range(8):
                    s.tr(sy, lnF[:, dc * TLOC:(dc + 1) * TLOC],
                         ln_tm[:, dc * 128:(dc + 1) * 128], "LS")
            ld0 = s.mark("LS")

            @block.gpsimd
            def _(g):
                s.fence(g, g.memset(zed[:, :], 0.0), "FEG")
                s.d(g, lst.rearrange("(p f) t -> p (f t)", p=128), zed[:, :], "LG")
            lst_zeroed = s.mark("LG")

            cv0 = s.mark("CV")
            ct0 = s.mark("CT")

            @block.vector
            def _(v):
                s.w(v, "LS", ld0)
                v.memset(lo_t[:, :], 0.0)
                v.memset(hi_t[:, :], 1.0)
                s.fence(v, v.engine_nop())
                for it in range(NBIS):
                    s.fence(v, v.tensor_add(mid_t[:, :], lo_t[:, :], hi_t[:, :]))
                    s.fence(v, v.tensor_scalar_mul(mid_t[:, :], mid_t[:, :], 0.5))
                    i = v.tensor_scalar(bjunk[:, 0:TLOC], bsc[:, :], mid_t[:, :],
                                        0.0, ALU.is_gt, ALU.add,
                                        accum_out=cnt_t[:, :])
                    i.then_inc(s.h["CV"], 1)
                    v.wait_ge(s.h["CT"], ct0 + it + 1)
                    s.fence(v, v.tensor_copy(cnt128[:, :], psb[:, :]))
                    s.fence(v, v.tensor_scalar(cond_f[:, :], cnt128[:, :],
                                               float(CSEL), None, ALU.is_ge))
                    s.fence(v, v.tensor_sub(tmp_t[:, :], mid_t[:, :], lo_t[:, :]))
                    s.fence(v, v.tensor_mul(tmp_t[:, :], tmp_t[:, :], cond_f[:, :]))
                    s.fence(v, v.tensor_add(lo_t[:, :], lo_t[:, :], tmp_t[:, :]))
                    s.fence(v, v.tensor_sub(tmp_t[:, :], hi_t[:, :], mid_t[:, :]))
                    s.fence(v, v.tensor_mul(tmp_t[:, :], tmp_t[:, :], cond_f[:, :]))
                    i = v.tensor_add(hi_t[:, :], mid_t[:, :], tmp_t[:, :])
                    i.then_inc(s.h["CV"], 1)
                    v.wait_ge(s.h["CV"], cv0 + 2 * it + 2)
            s.c["CV"] = cv0 + 2 * NBIS

            @block.tensor
            def _(t):
                s.w(t, "LS", ld0)
                for it in range(NBIS):
                    t.wait_ge(s.h["CV"], cv0 + 2 * it + 1)
                    if it >= 1:
                        t.wait_ge(s.h["CV"], cv0 + 2 * it)
                    i = t.matmul(psb[:, :], g8s[:, :], cnt_t[:, :],
                                 start=True, stop=True)
                    s.inc(i, "CT", 1)
            bis_cv = s.mark("CV")

            if stop_after < 2:
                raise _StopEmit(s, block)
            cvP2 = s.mark("CV")

            @block.vector
            def _(v):
                HT = TLOC // 2
                s.fence(v, v.tensor_scalar(ow_m[:, :], ow_s[:, :], lo_t[0:16, :],
                                           None, ALU.is_gt))
                s.fence(v, v.tensor_tensor_scan(ow_p[:, 0:HT], ow_m[:, 0:HT],
                                                ow_m[:, 0:HT], 0.0, ALU.add,
                                                ALU.bypass))
                s.fence(v, v.tensor_tensor_scan(ow_p[:, HT:], ow_m[:, HT:],
                                                ow_m[:, HT:], 0.0, ALU.add,
                                                ALU.bypass))
                s.fence(v, v.tensor_scalar(ow_t[:, HT:], ow_p[:, HT:],
                                           ow_p[:, HT - 1:HT], None, ALU.add))
                s.fence(v, v.tensor_copy(ow_t[:, 0:HT], ow_p[:, 0:HT]))
                s.fence(v, v.tensor_sub(ow_p[:, :], ow_t[:, :], ow_m[:, :]))
                s.fence(v, v.tensor_scalar(ow_t[:, :], ow_p[:, :], float(CAP),
                                           None, ALU.is_lt))
                s.fence(v, v.tensor_mul(ow_m[:, :], ow_m[:, :], ow_t[:, :]))
                s.fence(v, v.tensor_scalar(ow_p[:, :], ow_p[:, :], eb_s[:, :],
                                           None, ALU.add))
                s.fence(v, v.tensor_scalar_sub(ow_p[:, :], ow_p[:, :], OOBENC))
                s.fence(v, v.tensor_mul(ow_p[:, :], ow_p[:, :], ow_m[:, :]))
                i = v.tensor_scalar_add(enc[:, :], ow_p[:, :], OOBENC)
                i.then_inc(s.h["CV"], 1)
            s.c["CV"] = cvP2 + 1

            ctP2 = s.mark("CT")

            @block.tensor
            def _(t):
                s.w(t, "CV", cvP2 + 1)
                for c in range(32):
                    if c >= 2:
                        t.wait_ge(s.h["CV"], cvP2 + 1 + (c - 1))
                    i = t.transpose(rps[c % 2][:, :], enc[0:16, c * 128:(c + 1) * 128],
                                    idn[0:16, 0:16])
                    s.inc(i, "CT", 1)

            @block.vector
            def _(v):
                for c in range(32):
                    v.wait_ge(s.h["CT"], ctP2 + c + 1)
                    i = v.tensor_copy(rows_tm[:, c * E:(c + 1) * E], rps[c % 2][:, :])
                    i.then_inc(s.h["CV"], 1)
                    if c == 31:
                        v.wait_ge(s.h["CV"], cvP2 + 1 + 32)
            s.c["CV"] = cvP2 + 1 + 32
            cvP3 = s.mark("CV")

            @block.vector
            def _(v):
                s.fence(v, v.tensor_copy(rows_i[:, :], rows_tm[:, :]))
                s.fence(v, v.tensor_scalar(ri_a[:, :], rows_i[:, :], 127, None,
                                           ALU.bitwise_and))
                s.fence(v, v.tensor_scalar(ri_b[:, :], rows_i[:, :], 7, None,
                                           ALU.arith_shift_right))
                s.fence(v, v.tensor_scalar(ri_a[:, :], ri_a[:, :], 80, None,
                                           ALU.mult))
                i = v.tensor_tensor(rows_i[:, :], ri_a[:, :], ri_b[:, :], ALU.add)
                i.then_inc(s.h["CV"], 1)
                pv = pay.rearrange("p (c e t) -> p c e t", e=E, t=2)
                sv = stm.rearrange("p (c e) -> p c e", e=E)
                for e in range(E):
                    v.tensor_copy(pv[:, :, e, 0], ids_s[:, :])
                    i = v.tensor_copy(pv[:, :, e, 1], sv[:, :, e])
                i.then_inc(s.h["CV"], 1)
            s.c["CV"] = cvP3 + 2
            cvP4 = s.mark("CV")

            @block.gpsimd
            def _(g):
                s.w(g, "LG", lst_zeroed)
                s.w(g, "CV", cvP4)
                bc_reg = g.to_reg(NLIST - 1)
                for e in range(E):
                    for c in range(32):
                        k2 = c * E + e
                        s.idma(g, lst[:, :],
                               IndirectOffsetOnAxis(rows_i[:, k2:k2 + 1], 0),
                               pay[:, 2 * k2:2 * k2 + 2], None, "SC",
                               bounds_check=bc_reg, oob_is_err=False)
            sc_done = s.mark("SC")

            @block.gpsimd
            def _(g):
                s.w(g, "SC", sc_done)
                s.d(g, lst_sb[:, :], lst.rearrange("(p f) t -> p (f t)", p=128),
                    "LG")
            lst_ld = s.mark("LG")

            @block.vector
            def _(v):
                s.w(v, "LG", lst_ld)
                lv = lst_sb.rearrange("p (f t) -> p f t", t=2)
                v.tensor_copy(idx_i[:, :], lv[:, :, 0])
                i = v.tensor_copy(w_s[:, :], lv[:, :, 1])
                i.then_inc(s.h["CV"], 1)
            s.c["CV"] = cvP4 + 1
            sel_ready = s.mark("CV")

            es1.close(); _scopes.remove(es1)

            if stop_after < 3:
                raise _StopEmit(s, block)
            es2 = ExitStack(); _scopes.append(es2)
            sgt = [_sb(nc, es2, f"sgt{i}", [128, 8 * 128], BF16) for i in range(2)]
            sut = [_sb(nc, es2, f"sut{i}", [128, 8 * 128], BF16) for i in range(2)]
            sd_sb = _sb(nc, es2, "sd_sb", [128, 16 * D], BF16)
            gl = [_sb(nc, es2, f"gl{i}", [128, 1024], BF16) for i in range(2)]
            dsb = [_sb(nc, es2, f"dsb{i}", [128, D], F32) for i in range(2)]
            psg = [_psum(nc, es2, f"psg{i}", [128, 1024]) for i in range(2)]
            psu = [_psum(nc, es2, f"psu{i}", [128, 1024]) for i in range(2)]

            ct_sh = s.mark("CT")
            ca_sh = s.mark("CA")
            cv_sh = s.mark("CV")
            sh_mark = {}
            ac_mark = {}
            sd_mark = [0]

            def ct_up(tt, oc):
                return ct_sh + 24 * tt + oc + 1

            def ct_dn(tt, tc):
                return ct_sh + 24 * tt + 16 + tc + 1

            def ca_g(tt, oc):
                return ca_sh + 16 * tt + oc + 1

            def cv_stt(tt, oc):
                return cv_sh + 24 * tt + oc + 1

            def cv_ev(tt, tc):
                return cv_sh + 24 * tt + 16 + tc + 1

            @block.gpsimd
            def _(g):
                s.w(g, "CV", sel_ready)
                sd_mark[0] = s.d(g, sd_sb[:, :], sdT[:, :], "LG")

                def tile_pair(t2, oc):
                    k = 16 * t2 + oc
                    if k >= 2:
                        kk = k - 2
                        s.w(g, "CT", ct_up(kk // 16, kk % 16))
                    sem = "SH%d" % (oc % 2)
                    s.d(g, sgt[oc % 2][:, :].rearrange("p (dc i) -> p dc i", dc=8),
                        sgT[:, :, oc * 128:(oc + 1) * 128], sem)
                    sh_mark[(t2, oc)] = s.d(
                        g, sut[oc % 2][:, :].rearrange("p (dc i) -> p dc i", dc=8),
                        suT[:, :, oc * 128:(oc + 1) * 128], sem)

                for oc in range(16):
                    tile_pair(0, oc)
                for tt in range(4):
                    if tt + 1 < 4:
                        tile_pair(tt + 1, 0)
                        tile_pair(tt + 1, 1)
                    for tc in range(8):
                        s.w(g, "CV", cv_ev(tt, tc))
                        ch = tt * 8 + tc
                        i = g.dma_start(out=acc[ch * 128:(ch + 1) * 128, :],
                                        in_=dsb[tc % 2][:, :])
                        ac_mark[(tt, tc)] = s.inc(i, "AC%d" % (tc % 2), 16)
                    if tt + 1 < 4:
                        for oc in range(2, 16):
                            tile_pair(tt + 1, oc)

            @block.tensor
            def _(t):
                s.w(t, "LS", ld0)
                for tt in range(4):
                    for oc in range(16):
                        s.w(t, "SH%d" % (oc % 2), sh_mark[(tt, oc)])
                        if tt == 0 and oc == 0:
                            s.w(t, "LG", sd_mark[0])
                        if tt >= 1 and oc <= 1:
                            t.wait_ge(s.h["CV"], cv_ev(tt - 1, 7))
                        k = 16 * tt + oc
                        if k >= 2:
                            kk = k - 2
                            t.wait_ge(s.h["CA"], ca_g(kk // 16, kk % 16))
                            t.wait_ge(s.h["CV"], cv_stt(kk // 16, kk % 16))
                        for dc in range(8):
                            r0 = dc * TLOC + tt * 1024
                            t.matmul(psg[oc % 2][:, 0:512], sgt[oc % 2][:, dc * 128:(dc + 1) * 128],
                                     lnF[:, r0:r0 + 512], start=(dc == 0), stop=(dc == 7))
                            t.matmul(psg[oc % 2][:, 512:1024], sgt[oc % 2][:, dc * 128:(dc + 1) * 128],
                                     lnF[:, r0 + 512:r0 + 1024], start=(dc == 0), stop=(dc == 7))
                            t.matmul(psu[oc % 2][:, 0:512], sut[oc % 2][:, dc * 128:(dc + 1) * 128],
                                     lnF[:, r0:r0 + 512], start=(dc == 0), stop=(dc == 7))
                            i = t.matmul(psu[oc % 2][:, 512:1024], sut[oc % 2][:, dc * 128:(dc + 1) * 128],
                                         lnF[:, r0 + 512:r0 + 1024], start=(dc == 0), stop=(dc == 7))
                        s.inc(i, "CT", 1)
                    t.wait_ge(s.h["CV"], cv_stt(tt, 15))
                    t.wait_ge(s.h["CA"], ca_g(tt, 15))
                    for tc in range(8):
                        if tc >= 2:
                            t.wait_ge(s.h["CV"], cv_ev(tt, tc - 2))
                        for isc in range(16):
                            h0 = isc * 1024 + tc * 128
                            t.matmul(psg[tc % 2][:, 0:512], Hbuf[:, h0:h0 + 128],
                                     sd_sb[:, isc * 1024:isc * 1024 + 512],
                                     start=(isc == 0), stop=(isc == 15))
                            i = t.matmul(psg[tc % 2][:, 512:1024], Hbuf[:, h0:h0 + 128],
                                         sd_sb[:, isc * 1024 + 512:(isc + 1) * 1024],
                                         start=(isc == 0), stop=(isc == 15))
                        s.inc(i, "CT", 1)
            assert s.mark("CT") == ct_sh + 24 * 4

            @block.scalar
            def _(sc):
                for tt in range(4):
                    for oc in range(16):
                        s.w(sc, "CT", ct_up(tt, oc))
                        i = sc.activation(out=gl[oc % 2][:, :], in_=psg[oc % 2][:, :],
                                          func=AF.Gelu)
                        s.inc(i, "CA", 1)
            assert s.mark("CA") == ca_sh + 64

            @block.vector
            def _(v):
                for tt in range(4):
                    for oc in range(16):
                        v.wait_ge(s.h["CA"], ca_g(tt, oc))
                        if oc == 0 and tt >= 1:
                            v.wait_ge(s.h["CT"], ct_dn(tt - 1, 7))
                        i = v.scalar_tensor_tensor(Hbuf[:, oc * 1024:(oc + 1) * 1024],
                                                   psu[oc % 2][:, :], 1.0,
                                                   gl[oc % 2][:, :], ALU.mult, ALU.mult)
                        i.then_inc(s.h["CV"], 1)
                    for tc in range(8):
                        v.wait_ge(s.h["CT"], ct_dn(tt, tc))
                        if 8 * tt + tc >= 2:
                            k = 8 * tt + tc - 2
                            v.wait_ge(s.h["AC%d" % (k % 2)],
                                      ac_mark[(k // 8, k % 8)])
                        i = v.tensor_copy(dsb[tc % 2][:, :], psg[tc % 2][:, :])
                        i.then_inc(s.h["CV"], 1)
            s.c["CV"] = cv_sh + 24 * 4
            sh_done_ct = s.mark("CT")
            sh_done_ac = (s.mark("AC0"), s.mark("AC1"))

            if stop_after < 4:
                raise _StopEmit(s, block)
            es2.close(); _scopes.remove(es2)
            es_ln.close(); _scopes.remove(es_ln)
            es3 = ExitStack(); _scopes.append(es3)
            wgt = [_sb(nc, es3, f"wgt{i}", [128, 1024], BF16) for i in range(2)]
            wut = [_sb(nc, es3, f"wut{i}", [128, 1024], BF16) for i in range(2)]
            wd_all = _sb(nc, es3, "wd_all", [128, 32 * 1024], BF16)
            xsf = [_sb(nc, es3, f"xsf{i}", [128, 8 * CAP], BF16) for i in range(2)]
            xtm = [_sb(nc, es3, f"xtm{i}", [128, D], BF16) for i in range(2)]
            gle = [_sb(nc, es3, f"gle{i}", [128, CAP], BF16) for i in range(2)]
            eo_sb = [_sb(nc, es3, f"eo{i}", [128, D], BF16) for i in range(NSC)]
            psA = [_psum(nc, es3, f"psA{i}", [128, 1024]) for i in range(2)]
            psB = [_psum(nc, es3, f"psB{i}", [128, 1024]) for i in range(2)]

            ctE0 = s.mark("CT")
            caE0 = s.mark("CA")
            cvE0 = s.mark("CV")

            def ct_eup(e, ic):
                return ctE0 + 37 * e + ic + 1

            def ct_edn(e, sc):
                return ctE0 + 37 * e + 32 + sc + 1

            def ca_gel(e, ic):
                return caE0 + 32 * e + ic + 1

            def cv_h(e, ic):
                return cvE0 + 37 * e + ic + 1

            def cv_eo(e, sc):
                return cvE0 + 37 * e + 32 + sc + 1

            lg_gather = {}
            lg_sc = {}
            lg_wgt = {}
            lg_wd = {}
            trB = [s.mark("TR0"), s.mark("TR1")]

            @block.gpsimd
            def _(g):
                s.w(g, "CV", sel_ready)
                s.w(g, "CT", sh_done_ct)
                for gg in range(NSC):
                    w1 = s.idma(g, xtm[gg % 2][:, :], None, ln_tm[:, :],
                                IndirectOffsetOnAxis(idx_i[:, gg:gg + 1], 0), "GA0")
                    s.w(g, "GA0", w1)
                    lg_gather[(0, gg)] = s.d(
                        g, xsel_d[0][gg * 128:(gg + 1) * 128, :], xtm[gg % 2][:, :],
                        "GA0")
                for e in range(17):
                    if e >= 1:
                        ep = e - 1
                        if ep == 0:
                            s.w(g, "AC0", sh_done_ac[0])
                            s.w(g, "AC1", sh_done_ac[1])
                        if ep >= 1:
                            s.w(g, "SC", lg_sc[ep - 1])
                        s.w(g, "CV", cv_eo(ep, NSC - 1))
                        for sc in range(NSC):
                            if sc >= 1:
                                s.w(g, "SC", s.mark("SC"))
                            s.idma(g, acc[:, :],
                                   IndirectOffsetOnAxis(idx_i[:, 5 * ep + sc:5 * ep + sc + 1], 0),
                                   eo_sb[sc][:, :], None, "SC",
                                   compute_op=ALU.add)
                        lg_sc[ep] = s.mark("SC")
                    if e < 16:
                        if e == 0:
                            s.w(g, "CT", sh_done_ct)
                        for ic in range(32):
                            gidx = 32 * e + ic
                            if gidx >= 2:
                                ep2, ic2 = divmod(gidx - 2, 32)
                                s.w(g, "CT", ct_eup(ep2, ic2))
                            wsem = "WT%d" % (ic % 2)
                            s.d(g, wgt[ic % 2][:, :], wg_til[e, ic, :, :], wsem)
                            lg_wgt[(e, ic)] = s.d(g, wut[ic % 2][:, :],
                                                  wu_til[e, ic, :, :], wsem)
                        if e >= 1:
                            s.w(g, "CT", ct_edn(e - 1, NSC - 1))
                        for ic in range(32):
                            s.d(g, wd_all[:, ic * 1024:(ic + 1) * 1024],
                                wd_til[e, ic, :, :], "WD")
                        lg_wd[e] = s.mark("WD")
                        if e + 1 < 16:
                            b2 = (e + 1) % 2
                            gsem = "GA%d" % b2
                            if e >= 1:
                                s.w(g, "TR%d" % b2,
                                    trB[b2] + 128 * ((e - 1) // 2 + 1))
                            s.w(g, "GA%d" % (e % 2), lg_gather[(e, NSC - 1)])
                            for gg in range(NSC):
                                gk = 5 * (e + 1) + gg
                                w1 = s.idma(
                                    g, xtm[gg % 2][:, :], None, ln_tm[:, :],
                                    IndirectOffsetOnAxis(
                                        idx_i[:, gk:gk + 1], 0),
                                    gsem)
                                s.w(g, gsem, w1)
                                lg_gather[(e + 1, gg)] = s.d(
                                    g, xsel_d[b2][gg * 128:(gg + 1) * 128, :],
                                    xtm[gg % 2][:, :], gsem)
            lg_exp_done = s.mark("LG")

            tr_mark = {}

            @block.sync
            def _(sy):
                for e in range(16):
                    b = e % 2
                    s.w(sy, "GA%d" % b, lg_gather[(e, NSC - 1)])
                    if e == 0:
                        s.w(sy, "CT", sh_done_ct)
                    if e >= 2:
                        s.w(sy, "CT", ct_eup(e - 2, 31))
                    for dc in range(8):
                        s.tr(sy, xsf[b][:, dc * CAP:(dc + 1) * CAP],
                             xsel_d[b][:, dc * 128:(dc + 1) * 128], "TR%d" % b)
                    tr_mark[e] = s.mark("TR%d" % b)

            @block.tensor
            def _(t):
                for e in range(16):
                    b = e % 2
                    s.w(t, "TR%d" % b, tr_mark[e])
                    for ic in range(32):
                        s.w(t, "WT%d" % (ic % 2), lg_wgt[(e, ic)])
                        if e == 0 and ic == 0:
                            t.wait_ge(s.h["CV"], cvE0)
                            t.wait_ge(s.h["CA"], caE0)
                        if e >= 1 and ic <= 1:
                            t.wait_ge(s.h["CV"], cv_eo(e - 1, NSC - 1))
                        gidx = 32 * e + ic
                        if gidx >= 2:
                            ep2, ic2 = divmod(gidx - 2, 32)
                            t.wait_ge(s.h["CA"], ca_gel(ep2, ic2))
                            t.wait_ge(s.h["CV"], cv_h(ep2, ic2))
                        for dc in range(8):
                            w_sl = wgt[ic % 2][:, dc * 128:(dc + 1) * 128]
                            u_sl = wut[ic % 2][:, dc * 128:(dc + 1) * 128]
                            xa = xsf[b][:, dc * CAP:dc * CAP + 512]
                            xb = xsf[b][:, dc * CAP + 512:dc * CAP + CAP]
                            st_, sp_ = (dc == 0), (dc == 7)
                            t.matmul(psA[ic % 2][:, 0:512], w_sl, xa, start=st_, stop=sp_)
                            t.matmul(psA[ic % 2][:, 512:512 + 128], w_sl, xb, start=st_, stop=sp_)
                            t.matmul(psB[ic % 2][:, 0:512], u_sl, xa, start=st_, stop=sp_)
                            i = t.matmul(psB[ic % 2][:, 512:512 + 128], u_sl, xb, start=st_, stop=sp_)
                        s.inc(i, "CT", 1)
                    s.w(t, "WD", lg_wd[e])
                    t.wait_ge(s.h["CV"], cv_h(e, 31))
                    t.wait_ge(s.h["CA"], ca_gel(e, 31))
                    for sc in range(NSC):
                        if sc >= 2:
                            t.wait_ge(s.h["CV"], cv_eo(e, sc - 2))
                        for isc in range(32):
                            h0 = isc * CAP + sc * 128
                            t.matmul(psA[sc % 2][:, 0:512], Hbuf[:, h0:h0 + 128],
                                     wd_all[:, isc * 1024:isc * 1024 + 512],
                                     start=(isc == 0), stop=(isc == 31))
                            i = t.matmul(psA[sc % 2][:, 512:1024], Hbuf[:, h0:h0 + 128],
                                         wd_all[:, isc * 1024 + 512:(isc + 1) * 1024],
                                         start=(isc == 0), stop=(isc == 31))
                        s.inc(i, "CT", 1)
            assert s.mark("CT") == ctE0 + 37 * 16

            @block.scalar
            def _(sc_):
                for e in range(16):
                    for ic in range(32):
                        sc_.wait_ge(s.h["CT"], ct_eup(e, ic))
                        i = sc_.activation(out=gle[ic % 2][:, :],
                                           in_=psA[ic % 2][:, 0:CAP], func=AF.Gelu)
                        s.inc(i, "CA", 1)
            assert s.mark("CA") == caE0 + 32 * 16

            @block.vector
            def _(v):
                for e in range(16):
                    for ic in range(32):
                        v.wait_ge(s.h["CA"], ca_gel(e, ic))
                        if e == 0 and ic == 0:
                            v.wait_ge(s.h["CT"], sh_done_ct)
                        i = v.scalar_tensor_tensor(
                            Hbuf[:, ic * CAP:(ic + 1) * CAP],
                            psB[ic % 2][:, 0:CAP], 1.0, gle[ic % 2][:, :],
                            ALU.mult, ALU.mult)
                        i.then_inc(s.h["CV"], 1)
                    for sc in range(NSC):
                        v.wait_ge(s.h["CT"], ct_edn(e, sc))
                        if e == 0 and sc == 0:
                            v.wait_ge(s.h["AC0"], sh_done_ac[0])
                            v.wait_ge(s.h["AC1"], sh_done_ac[1])
                        if e >= 1 and sc == 0:
                            v.wait_ge(s.h["SC"], lg_sc[e - 1])
                        i = v.tensor_scalar(eo_sb[sc][:, :], psA[sc % 2][:, :],
                                            w_s[:, 5 * e + sc:5 * e + sc + 1], None,
                                            ALU.mult)
                        i.then_inc(s.h["CV"], 1)
            s.c["CV"] = cvE0 + 37 * 16
            exp_done_cv = s.mark("CV")

            if stop_after < 5:
                raise _StopEmit(s, block)
            es3.close(); _scopes.remove(es3)
            es3b = ExitStack(); _scopes.append(es3b)
            asb = [_sb(nc, es3b, f"fas{i}", [128, D], F32) for i in range(2)]
            xsb = [_sb(nc, es3b, f"fxs{i}", [128, D], F32) for i in range(2)]
            osb = [_sb(nc, es3b, f"fos{i}", [128, D], F32) for i in range(2)]
            cvF0 = s.mark("CV")
            stFB = [s.mark("ST0"), s.mark("ST1")]

            fl_mark = {}

            @block.sync
            def _(sy):
                s.w(sy, "SC", lg_sc[15])
                s.w(sy, "AC0", sh_done_ac[0])
                s.w(sy, "AC1", sh_done_ac[1])
                for c in range(32):
                    fsem = "LSF%d" % (c % 2)
                    if c >= 2:
                        s.w(sy, "CV", cvF0 + (c - 2) + 1)
                    s.d(sy, asb[c % 2][:, :], acc[c * 128:(c + 1) * 128, :], fsem)
                    fl_mark[c] = s.d(sy, xsb[c % 2][:, :],
                                     x_in[c * 128:(c + 1) * 128, :], fsem)

            @block.vector
            def _(v):
                for c in range(32):
                    v.wait_ge(s.h["LSF%d" % (c % 2)], fl_mark[c])
                    if c >= 2:
                        bb = c % 2
                        v.wait_ge(s.h["ST%d" % bb], stFB[bb] + 16 * ((c - 2) // 2 + 1))
                    i = v.scalar_tensor_tensor(osb[c % 2][:, :], asb[c % 2][:, :],
                                               0.5, xsb[c % 2][:, :],
                                               ALU.mult, ALU.add)
                    i.then_inc(s.h["CV"], 1)
            s.c["CV"] = cvF0 + 32

            @block.gpsimd
            def _(g):
                for c in range(32):
                    s.w(g, "CV", cvF0 + c + 1)
                    s.d(g, x_next[c * 128:(c + 1) * 128, :], osb[c % 2][:, :],
                        "ST%d" % (c % 2))
            fin_st = ("ST0", s.mark("ST0"), "ST1", s.mark("ST1"))

            if stop_after < 6:
                raise _StopEmit(s, block)
            if with_qkv:
                es3b.close(); _scopes.remove(es3b)
                es4 = ExitStack()
                with es4:
                    ln2_done = emit_ln(nc, es4, block, s, x_next, ln2_d, lo2_d,
                                       pre=fin_st)
                es5 = ExitStack(); _scopes.append(es5)
                lnF2 = _sb(nc, es5, "lnF2", [128, 8 * TLOC], BF16)
                wq_sb = _sb(nc, es5, "wq_sb", [128, 8 * D], BF16)
                wk_sb = _sb(nc, es5, "wk_sb", [128, 8 * D], BF16)
                wv_sb = _sb(nc, es5, "wv_sb", [128, 8 * D], BF16)
                pq = [_psum(nc, es5, f"pq{i}", [128, 512]) for i in range(2)]
                pv = [_psum(nc, es5, f"pv{i}", [128, 1024]) for i in range(2)]
                qsb = [_sb(nc, es5, f"qsb{i}", [128, 512], BF16) for i in range(2)]
                vsb = [_sb(nc, es5, f"vsb{i}", [128, 1024], BF16) for i in range(2)]

                @block.sync
                def _(sy):
                    for j in range(0, len(ln2_done), 2):
                        s.w(sy, ln2_done[j], ln2_done[j + 1])
                    s.w(sy, "CT", s.mark("CT"))
                    for dc in range(8):
                        s.tr(sy, lnF2[:, dc * TLOC:(dc + 1) * TLOC],
                             ln2_d[:, dc * 128:(dc + 1) * 128], "LS")
                lnF2_ls = s.mark("LS")

                @block.gpsimd
                def _(g):
                    s.d(g, wq_sb[:, :], wqT[:, :], "LG")
                    s.d(g, wk_sb[:, :], wkT[:, :], "LG")
                    s.d(g, wv_sb[:, :], wvT[:, :], "LG")
                qkvw = s.mark("LG")

                ctQ0 = s.mark("CT")
                cvQ0 = s.mark("CV")
                stQB = [s.mark("ST0"), s.mark("ST1")]

                @block.tensor
                def _(t):
                    s.w(t, "LS", lnF2_ls)
                    s.w(t, "LG", qkvw)
                    n = 0
                    for wsb in (wq_sb, wk_sb):
                        for oc in range(8):
                            for th in range(8):
                                if n >= 2:
                                    t.wait_ge(s.h["CV"], cvQ0 + n - 1)
                                for dc in range(8):
                                    i = t.matmul(
                                        pq[n % 2][:, :],
                                        wsb[:, dc * D + oc * 128:dc * D + (oc + 1) * 128],
                                        lnF2[:, dc * TLOC + th * 512:dc * TLOC + (th + 1) * 512],
                                        start=(dc == 0), stop=(dc == 7))
                                s.inc(i, "CT", 1)
                                n += 1
                    for tc in range(32):
                        if tc >= 2:
                            t.wait_ge(s.h["CV"], cvQ0 + 128 + tc - 1)
                        for dc in range(8):
                            ls_ = lnF2[:, dc * TLOC + tc * 128:dc * TLOC + (tc + 1) * 128]
                            t.matmul(pv[tc % 2][:, 0:512], ls_,
                                     wv_sb[:, dc * D:dc * D + 512],
                                     start=(dc == 0), stop=(dc == 7))
                            i = t.matmul(pv[tc % 2][:, 512:1024], ls_,
                                         wv_sb[:, dc * D + 512:(dc + 1) * D],
                                         start=(dc == 0), stop=(dc == 7))
                        s.inc(i, "CT", 1)

                @block.vector
                def _(v):
                    for n in range(128):
                        v.wait_ge(s.h["CT"], ctQ0 + n + 1)
                        if n >= 2:
                            bb = n % 2
                            v.wait_ge(s.h["ST%d" % bb],
                                      stQB[bb] + 16 * ((n - 2) // 2 + 1))
                        i = v.tensor_copy(qsb[n % 2][:, :], pq[n % 2][:, :])
                        i.then_inc(s.h["CV"], 1)
                    for tc in range(32):
                        v.wait_ge(s.h["CT"], ctQ0 + 128 + tc + 1)
                        if tc >= 2:
                            bb = tc % 2
                            v.wait_ge(s.h["ST%d" % bb],
                                      stQB[bb] + 16 * 64 + 16 * ((tc - 2) // 2 + 1))
                        i = v.tensor_copy(vsb[tc % 2][:, :], pv[tc % 2][:, :])
                        i.then_inc(s.h["CV"], 1)
                s.c["CV"] = cvQ0 + 160

                @block.sync
                def _(sy):
                    n = 0
                    for out_d in (q_f, k_f):
                        for oc in range(8):
                            for th in range(8):
                                s.w(sy, "CV", cvQ0 + n + 1)
                                s.d(sy, out_d[oc * 128:(oc + 1) * 128,
                                              th * 512:(th + 1) * 512],
                                    qsb[n % 2][:, :], "ST%d" % (n % 2))
                                n += 1
                    for tc in range(32):
                        s.w(sy, "CV", cvQ0 + 128 + tc + 1)
                        s.d(sy, v_tm[tc * 128:(tc + 1) * 128, :], vsb[tc % 2][:, :],
                            "ST%d" % (tc % 2))

            @block.sync
            def _(sy):
                for nm in ("LS", "LG", "ST0", "ST1", "ES0", "ES1", "AC0", "AC1",
                           "SC", "WT0", "WT1", "WD", "GA0", "GA1", "SH0", "SH1",
                           "TR0", "TR1", "LSF0", "LSF1", "EX0", "EX1"):
                    s.w(sy, nm, s.mark(nm))
            if with_qkv:
                es5.close(); _scopes.remove(es5)
            else:
                es3b.close(); _scopes.remove(es3b)
          except _StopEmit:
            for _sc_ in reversed(_scopes):
                _sc_.close()
            _scopes.clear()

            names = list(s.h)

            @block.sync
            def _(sy):
                for nm in names:
                    if nm not in ("CT", "CA", "FEV", "FEG"):
                        s.w(sy, nm, s.mark(nm))

            @block.vector
            def _(v):
                s.w(v, "CT", s.mark("CT"))
                s.w(v, "CA", s.mark("CA"))
                s.w(v, "CV", s.mark("CV"))
    return nc



def build_L3():
    nc = bass.Bass("TRN2", target_bir_lowering=False)
    x1 = nc.dram_tensor("x1", [TLOC, D], F32, kind="ExternalInput").ap()
    q_in = nc.dram_tensor("q_in", [D, TLOC], BF16, kind="ExternalInput").ap()
    k_pair = nc.dram_tensor("k_pair", [D, 2 * TLOC], BF16, kind="ExternalInput").ap()
    v_pair = nc.dram_tensor("v_pair", [2 * TLOC, D], BF16, kind="ExternalInput").ap()
    woT = nc.dram_tensor("woT", [128, 8 * D], F32, kind="ExternalInput").ap()
    gwT = nc.dram_tensor("gwT", [D, E], F32, kind="ExternalInput").ap()
    ones1 = nc.dram_tensor("ones1", [1, 128], F32, kind="ExternalInput").ap()
    x2 = nc.dram_tensor("x2", [TLOC, D], F32, kind="ExternalOutput").ap()
    ln3 = nc.dram_tensor("ln3", [TLOC, D], BF16, kind="ExternalOutput").ap()
    p3 = nc.dram_tensor("p3", [TLOC, E], F32, kind="ExternalOutput").ap()
    lo3 = nc.dram_tensor("lo3", [TLOC, D], BF16, kind="Internal").ap()

    SK = 2 * TLOC
    NKC = SK // 128

    with ExitStack() as es:
        s = S(nc, es, ["LS", "LG", "CV", "CA", "CT", "ST", "ST0", "ST1", "FEV", "FEG", "EX0", "EX1", "ES0", "ES1", "AC0", "AC1", "SC", "WT0", "WT1", "WD", "GA0", "GA1", "SH0", "SH1", "TR0", "TR1", "LSF0", "LSF1", "KV0", "KV1", "XL0", "XL1"])
        ones_s = _sb(nc, es, "ones_s", [1, 128], F32)

        with nc.Block() as block:
            esA = ExitStack()
            ctxF = _sb(nc, esA, "ctxF", [128, 8 * TLOC], BF16)
            wo_sb = _sb(nc, esA, "wo_sb", [128, 8 * D], BF16)
            ksb = [_sb(nc, esA, f"ksb{i}", [128, SK], BF16) for i in range(2)]
            qsb = [_sb(nc, esA, f"qsb{i}", [128, TLOC], BF16) for i in range(2)]
            vbuf = [_sb(nc, esA, f"vbuf{i}", [128, NKC * 130], BF16) for i in range(2)]
            wts = [_sb(nc, esA, f"wts{i}", [128, 1024], BF16) for i in range(4)]
            sxA = _sb(nc, esA, "sxA", [128, 512], F32)
            sxB = _sb(nc, esA, "sxB", [128, 512], F32)
            scr1 = _sb(nc, esA, "scr1", [1, 1], F32)
            recA = _sb(nc, esA, "recA", [1, 512], F32)
            recB = _sb(nc, esA, "recB", [1, 512], F32)
            bcsA = _sb(nc, esA, "bcsA", [128, 512], F32)
            bcsB = _sb(nc, esA, "bcsB", [128, 512], F32)
            pss = [_psum(nc, esA, f"pss{i}", [128, 1024]) for i in range(2)]
            psxA = _psum(nc, esA, "psxA", [128, 512])
            psxB = _psum(nc, esA, "psxB", [128, 512])
            bcpA = _psum(nc, esA, "bcpA", [128, 512])
            bcpB = _psum(nc, esA, "bcpB", [128, 512])

            def ct_cx(hp, qt):
                return 66 * (8 * hp + qt) + 65

            def ca_sg(hp, qt, kc):
                return 64 * (8 * hp + qt) + kc + 1

            def cv_at(n, k):
                return cvA0 + 4 * n + k

            ct_sc_mark = {}
            ct_cx_mark = {}
            ct_bc_mark = {}

            @block.gpsimd
            def _(g):
                s.d(g, wo_sb[:, :], woT[:, :], "LG")
            wo_ld = s.mark("LG")

            ld_hp = {}

            @block.sync
            def _(sy):
                ones_ld = s.d(sy, ones_s[:, :], ones1[:, :], "LS")
                ld_hp["ones"] = ones_ld
                for hp in range(8):
                    b = hp % 2
                    kv = "KV%d" % b
                    if hp >= 2:
                        sy.wait_ge(s.h["CT"], ct_cx(hp - 2, 7))
                    s.d(sy, ksb[b][:, :], k_pair[hp * 128:(hp + 1) * 128, :], kv)
                    s.d(sy, qsb[b][:, :], q_in[hp * 128:(hp + 1) * 128, :], kv)
                    for kc in range(NKC):
                        dst = vbuf[b].rearrange("p (c f) -> p c f", f=130)
                        s.d(sy, dst[:, kc, 0:129].rearrange("p f -> p f").rearrange(
                            "p (two s) -> p two s", two=1)
                            if False else
                            vbuf[b][:, kc * 130:kc * 130 + 129].rearrange(
                                "p (two s) -> p two s", s=64, two=2)
                            if False else dst[:, kc, :][:, 0:64], None, "LS") \
                            if False else None
                        i = sy.dma_start(
                            out=vbuf[b][:, kc * 130:kc * 130 + 130].rearrange(
                                "p (g c) -> p g c", g=2, c=65)[:, :, 0:64],
                            in_=v_pair[kc * 128:(kc + 1) * 128,
                                       hp * 128:(hp + 1) * 128].rearrange(
                                "k (g c) -> k g c", g=2, c=64))
                        s.inc(i, kv, 16)
                    ld_hp[hp] = s.mark(kv)

            @block.vector
            def _(v):
                for i in range(2):
                    vv = vbuf[i].rearrange("p (c f) -> p c f", f=130)
                    v.memset(vv[:, :, 64:65], 1.0)
                    j = v.memset(vv[:, :, 129:130], 1.0)
                s.inc(j, "CV", 1)
            ones_cv = s.mark("CV")
            cvA0 = s.mark("CV")

            def emit_bc(t, n):
                t.wait_ge(s.h["CV"], cv_at(n, 2))
                if n >= 1:
                    t.wait_ge(s.h["CV"], cv_at(n - 1, 3))
                t.matmul(bcpA[:, :], ones_s[:, :], recA[:, :],
                         start=True, stop=True)
                i = t.matmul(bcpB[:, :], ones_s[:, :], recB[:, :],
                             start=True, stop=True)
                ct_bc_mark[n] = s.inc(i, "CT", 1)

            @block.tensor
            def _(t):
                s.w(t, "CV", ones_cv)
                s.w(t, "LS", ld_hp["ones"])
                for hp in range(8):
                    b = hp % 2
                    s.w(t, "KV%d" % b, ld_hp[hp])
                    for qt in range(8):
                        n = 8 * hp + qt
                        for kc in range(NKC):
                            if kc >= 2:
                                t.wait_ge(s.h["CA"], ca_sg(hp, qt, kc - 2))
                            elif n >= 1:
                                t.wait_ge(s.h["CA"], 64 * (n - 1) + (62 + kc) + 1)
                            t.matmul(pss[kc % 2][:, 0:512],
                                     ksb[b][0:64, kc * 128:(kc + 1) * 128],
                                     qsb[b][0:64, qt * 512:(qt + 1) * 512],
                                     start=True, stop=True)
                            i = t.matmul(pss[kc % 2][:, 512:1024],
                                         ksb[b][64:128, kc * 128:(kc + 1) * 128],
                                         qsb[b][64:128, qt * 512:(qt + 1) * 512],
                                         start=True, stop=True)
                            ct_sc_mark[(n, kc)] = s.inc(i, "CT", 1)
                            if kc >= 1:
                                if kc == 1 and n >= 1:
                                    t.wait_ge(s.h["CV"], cv_at(n - 1, 1))
                                t.wait_ge(s.h["CA"], ca_sg(hp, qt, kc - 1))
                                kv = kc - 1
                                t.matmul(psxA[0:65, :],
                                         vbuf[b][:, kv * 130:kv * 130 + 65],
                                         wts[kv % 4][:, 0:512],
                                         start=(kv == 0), stop=False)
                                t.matmul(psxB[0:65, :],
                                         vbuf[b][:, kv * 130 + 65:kv * 130 + 130],
                                         wts[kv % 4][:, 512:1024],
                                         start=(kv == 0), stop=False)
                            if kc == 16 and n >= 1:
                                emit_bc(t, n - 1)
                        t.wait_ge(s.h["CA"], ca_sg(hp, qt, NKC - 1))
                        kv = NKC - 1
                        t.matmul(psxA[0:65, :], vbuf[b][:, kv * 130:kv * 130 + 65],
                                 wts[kv % 4][:, 0:512], start=False, stop=True)
                        i = t.matmul(psxB[0:65, :], vbuf[b][:, kv * 130 + 65:kv * 130 + 130],
                                     wts[kv % 4][:, 512:1024], start=False, stop=True)
                        ct_cx_mark[n] = s.inc(i, "CT", 1)
                        assert ct_cx_mark[n] == ct_cx(hp, qt)
                        if n == 63:
                            emit_bc(t, 63)
            assert s.mark("CT") == 66 * 64

            @block.scalar
            def _(sc):
                for hp in range(8):
                    for qt in range(8):
                        n = 8 * hp + qt
                        for kc in range(NKC):
                            sc.wait_ge(s.h["CT"], ct_sc_mark[(n, kc)])
                            i = sc.activation(out=wts[kc % 4][:, :],
                                              in_=pss[kc % 2][:, :],
                                              func=AF.Sigmoid, scale=ATT_SCALE)
                            s.inc(i, "CA", 1)
            assert s.mark("CA") == 64 * 64

            @block.vector
            def _(v):
                for hp in range(8):
                    for qt in range(8):
                        n = 8 * hp + qt
                        v.wait_ge(s.h["CT"], ct_cx_mark[n])
                        s.fence(v, v.tensor_copy(sxA[0:65, :], psxA[0:65, :]))
                        s.fence(v, v.tensor_copy(sxB[0:65, :], psxB[0:65, :]))
                        i = v.memset(scr1[:, :], 0.0)
                        i.then_inc(s.h["CV"], 1)
                        s.fence(v, v.tensor_scalar_add(recA[:, :], sxA[64:65, :],
                                                       EPS_ATT))
                        s.fence(v, v.tensor_scalar_add(recB[:, :], sxB[64:65, :],
                                                       EPS_ATT))
                        s.fence(v, v.reciprocal(recA[:, :], recA[:, :]))
                        i = v.reciprocal(recB[:, :], recB[:, :])
                        i.then_inc(s.h["CV"], 1)
                        v.wait_ge(s.h["CT"], ct_bc_mark[n])
                        s.fence(v, v.tensor_copy(bcsA[:, :], bcpA[:, :]))
                        s.fence(v, v.tensor_copy(bcsB[:, :], bcpB[:, :]))
                        i = v.memset(scr1[:, :], 0.0)
                        i.then_inc(s.h["CV"], 1)
                        i = v.scalar_tensor_tensor(
                            ctxF[:, hp * TLOC + qt * 512:hp * TLOC + (qt + 1) * 512][0:64, :],
                            sxA[0:64, :], 1.0, bcsA[0:64, :], ALU.mult, ALU.mult)
                        i = v.scalar_tensor_tensor(
                            ctxF[:, hp * TLOC + qt * 512:hp * TLOC + (qt + 1) * 512][64:128, :],
                            sxB[0:64, :], 1.0, bcsB[0:64, :], ALU.mult, ALU.mult)
                        i.then_inc(s.h["CV"], 1)
            s.c["CV"] = cvA0 + 4 * 64
            att_cv = s.mark("CV")

            x1b = [_sb(nc, esA, f"x1b{i}", [128, D], F32) for i in range(2)]
            x2b = [_sb(nc, esA, f"x2b{i}", [128, D], F32) for i in range(2)]
            cvO0 = s.mark("CV")
            ctO0 = s.mark("CT")
            stOB = [s.mark("ST0"), s.mark("ST1")]
            xl_mark = {}

            @block.sync
            def _(sy):
                for qc in range(32):
                    if qc >= 2:
                        s.w(sy, "CV", cvO0 + (qc - 2) + 1)
                    xl_mark[qc] = s.d(sy, x1b[qc % 2][:, :],
                                      x1[qc * 128:(qc + 1) * 128, :],
                                      "XL%d" % (qc % 2))

            @block.tensor
            def _(t):
                s.w(t, "LG", wo_ld)
                s.w(t, "CA", 64 * 64)
                s.w(t, "CV", att_cv)
                for qc in range(32):
                    if qc >= 2:
                        t.wait_ge(s.h["CV"], cvO0 + (qc - 2) + 1)
                    for dc in range(8):
                        c_sl = ctxF[:, dc * TLOC + qc * 128:dc * TLOC + (qc + 1) * 128]
                        t.matmul(pss[qc % 2][:, 0:512], c_sl,
                                 wo_sb[:, dc * D:dc * D + 512],
                                 start=(dc == 0), stop=(dc == 7))
                        i = t.matmul(pss[qc % 2][:, 512:1024], c_sl,
                                     wo_sb[:, dc * D + 512:(dc + 1) * D],
                                     start=(dc == 0), stop=(dc == 7))
                    s.inc(i, "CT", 1)

            @block.vector
            def _(v):
                for qc in range(32):
                    v.wait_ge(s.h["CT"], ctO0 + qc + 1)
                    v.wait_ge(s.h["XL%d" % (qc % 2)], xl_mark[qc])
                    if qc >= 2:
                        bb = qc % 2
                        v.wait_ge(s.h["ST%d" % bb],
                                  stOB[bb] + 16 * ((qc - 2) // 2 + 1))
                    i = v.tensor_add(x2b[qc % 2][:, :], pss[qc % 2][:, :],
                                     x1b[qc % 2][:, :])
                    i.then_inc(s.h["CV"], 1)
            s.c["CV"] = cvO0 + 32

            @block.gpsimd
            def _(g):
                for qc in range(32):
                    s.w(g, "CV", cvO0 + qc + 1)
                    s.d(g, x2[qc * 128:(qc + 1) * 128, :], x2b[qc % 2][:, :],
                        "ST%d" % (qc % 2))
            x2_st = ("ST0", s.mark("ST0"), "ST1", s.mark("ST1"))

            esA.close()
            es4 = ExitStack()
            with es4:
                ln3_done = emit_ln(nc, es4, block, s, x2, ln3, lo3,
                                   pre=x2_st)
            es5 = ExitStack()
            with es5:
                p3_done = emit_gate(nc, es5, block, s, ln3, lo3, gwT, p3, ln3_done)

            @block.sync
            def _(sy):
                for j in range(0, len(p3_done), 2):
                    s.w(sy, p3_done[j], p3_done[j + 1])
                for nm in ("LS", "LG", "ST0", "ST1", "ES0", "ES1", "EX0", "EX1",
                           "KV0", "KV1", "XL0", "XL1"):
                    s.w(sy, nm, s.mark(nm))
    return nc



_CACHE = {}


def _get_programs():
    if "L1" not in _CACHE:
        _CACHE["L1"] = build_L1()
        _CACHE["L2"] = build_FFN(True)
        _CACHE["L3"] = build_L3()
        _CACHE["L4"] = build_FFN(False)
    return _CACHE["L1"], _CACHE["L2"], _CACHE["L3"], _CACHE["L4"]


def _prep(inputs):
    h = {}
    x = np.ascontiguousarray(np.asarray(inputs["L0_attn_x"], dtype=np.float32))
    h["x0"] = x.reshape(NCORES * TLOC, D)
    h["gwT"] = np.ascontiguousarray(np.asarray(inputs["L0_ffn_gate_w"],
                                               dtype=np.float32).T)
    for nm, key in (("wqT", "L0_attn_Wq"), ("wkT", "L0_attn_Wk"),
                    ("wvT", "L0_attn_Wv"), ("woT", "L0_attn_Wo")):
        w = np.asarray(inputs[key], dtype=np.float32)
        h[nm] = np.ascontiguousarray(w.T.reshape(8, 128, D).transpose(1, 0, 2)
                                     ).reshape(128, 8 * D)
    wg = np.asarray(inputs["L0_ffn_Wg"], dtype=np.float32)
    wu = np.asarray(inputs["L0_ffn_Wu"], dtype=np.float32)
    wd = np.asarray(inputs["L0_ffn_Wd"], dtype=np.float32)
    h["wg_til"] = np.ascontiguousarray(
        wg.reshape(E, 32, 128, 8, 128).transpose(0, 1, 4, 3, 2)
    ).reshape(E, 32, 128, 8 * 128)
    h["wu_til"] = np.ascontiguousarray(
        wu.reshape(E, 32, 128, 8, 128).transpose(0, 1, 4, 3, 2)
    ).reshape(E, 32, 128, 8 * 128)
    h["wd_til"] = np.ascontiguousarray(
        wd.reshape(E, 8, 128, 32, 128).transpose(0, 3, 4, 1, 2)
    ).reshape(E, 32, 128, 8 * 128)
    sg = np.asarray(inputs["L0_ffn_Sg"], dtype=np.float32)
    su = np.asarray(inputs["L0_ffn_Su"], dtype=np.float32)
    sd = np.asarray(inputs["L0_ffn_Sd"], dtype=np.float32)
    h["sgT"] = np.ascontiguousarray(sg.T.reshape(8, 128, ISH).transpose(1, 0, 2))
    h["suT"] = np.ascontiguousarray(su.T.reshape(8, 128, ISH).transpose(1, 0, 2))
    h["sdT"] = np.ascontiguousarray(sd.T.reshape(16, 128, D).transpose(1, 0, 2)
                                    ).reshape(128, 16 * D)
    h["ident"] = np.eye(128, dtype=np.float32)
    h["ids32"] = (np.arange(32, dtype=np.float32)[None, :] * 128
                  + np.arange(128, dtype=np.float32)[:, None]).astype(np.float32)
    h["ebase"] = (CAP * (np.arange(128) % E)).astype(np.float32).reshape(128, 1)
    pp = np.arange(128)
    h["g8"] = (pp[:, None] % 16 == pp[None, :] % 16).astype(np.float32)
    h["ones1"] = np.ones((1, 128), np.float32)
    return h


def _p_layouts(p_full):
    b_all = np.ascontiguousarray(
        p_full.reshape(8, TLOC, E).transpose(0, 2, 1).reshape(128, TLOC))
    outs = []
    for c in range(NCORES):
        p_own = p_full[c * TLOC:(c + 1) * TLOC]
        b_own = np.ascontiguousarray(p_own.T)
        s_tm = np.ascontiguousarray(
            p_own.reshape(32, 128, E).transpose(1, 0, 2).reshape(128, 32 * E))
        outs.append((b_all, b_own, s_tm))
    return outs


def kernel(**inputs):
    h = _prep(inputs)
    L1, L2, L3, L4 = _get_programs()
    cores = list(range(NCORES))
    xs = [np.ascontiguousarray(h["x0"][c * TLOC:(c + 1) * TLOC]) for c in cores]

    r1 = run_bass_kernel_spmd(
        L1, [{"x0": xs[c], "gwT": h["gwT"]} for c in cores], core_ids=cores)
    p1_full = np.concatenate([np.asarray(r1.results[c]["p1"]) for c in cores], 0)
    ln1 = [np.asarray(r1.results[c]["ln1"]) for c in cores]

    lay1 = _p_layouts(p1_full)
    common = dict(wg_til=h["wg_til"], wu_til=h["wu_til"], wd_til=h["wd_til"],
                  sgT=h["sgT"], suT=h["suT"], sdT=h["sdT"], ids32=h["ids32"],
                  ebase=h["ebase"], g8=h["g8"], ident=h["ident"])
    in2 = []
    for c in cores:
        b_all, b_own, s_tm = lay1[c]
        in2.append(dict(x_in=xs[c], ln_tm=ln1[c], b_in=b_all, b_own=b_own,
                        s_tm=s_tm, wqT=h["wqT"], wkT=h["wkT"], wvT=h["wvT"],
                        **common))
    r2 = run_bass_kernel_spmd(L2, in2, core_ids=cores)
    x1 = [np.asarray(r2.results[c]["x_next"]) for c in cores]
    qf = [np.asarray(r2.results[c]["q_f"]) for c in cores]
    kf = [np.asarray(r2.results[c]["k_f"]) for c in cores]
    vt = [np.asarray(r2.results[c]["v_tm"]) for c in cores]

    in3 = []
    for c in cores:
        pr = c ^ 1
        a, b = (c, pr) if c % 2 == 0 else (pr, c)
        k_pair = np.concatenate([kf[a], kf[b]], axis=1)
        v_pair = np.concatenate([vt[a], vt[b]], axis=0)
        in3.append(dict(x1=x1[c], q_in=qf[c], k_pair=k_pair, v_pair=v_pair,
                        woT=h["woT"], gwT=h["gwT"], ones1=h["ones1"]))
    r3 = run_bass_kernel_spmd(L3, in3, core_ids=cores)
    x2 = [np.asarray(r3.results[c]["x2"]) for c in cores]
    ln3 = [np.asarray(r3.results[c]["ln3"]) for c in cores]
    p3_full = np.concatenate([np.asarray(r3.results[c]["p3"]) for c in cores], 0)

    lay3 = _p_layouts(p3_full)
    in4 = []
    for c in cores:
        b_all, b_own, s_tm = lay3[c]
        in4.append(dict(x_in=x2[c], ln_tm=ln3[c], b_in=b_all, b_own=b_own,
                        s_tm=s_tm, **common))
    r4 = run_bass_kernel_spmd(L4, in4, core_ids=cores)
    out = np.concatenate([np.asarray(r4.results[c]["x_next"]) for c in cores], 0)
    return out.reshape(4, NCORES * TLOC // 4 // 1024 * 1024 * 0 + 8192, D) \
        if False else out.reshape(4, 8192, D).astype(np.float32)
